# version 01
# baseline (speedup 1.0000x reference)
import numpy as np
import ml_dtypes
from contextlib import ExitStack
import concourse.bass as bass
import concourse.mybir as mybir
from concourse.bass_utils import run_bass_kernel_spmd

F32 = mybir.dt.float32
BF16 = mybir.dt.bfloat16
I32 = mybir.dt.int32
AF = mybir.ActivationFunctionType
ALU = mybir.AluOpType
BF16NP = ml_dtypes.bfloat16

NCORES = 8
D = 4096
B_SZ = 4
S = 4096
TC = 2048
EPS = 1e-5


def build_ln_launch(with_rms: bool, ntiles: int = TC // 128):
    nc = bass.Bass("TRN2", target_bir_lowering=False)
    xa = nc.dram_tensor("xa", [ntiles * 128, D], F32, kind="ExternalInput").ap()
    if with_rms:
        xb = nc.dram_tensor("xb", [ntiles * 128, D], F32, kind="ExternalInput").ap()
        xs = nc.dram_tensor("xs", [ntiles * 128, D], F32, kind="ExternalOutput").ap()
        h = nc.dram_tensor("h", [ntiles * 128, D], BF16, kind="ExternalOutput").ap()
    xn = nc.dram_tensor("xn", [ntiles * 128, D], BF16, kind="ExternalOutput").ap()

    nst = 3 if with_rms else 1
    nld = 2 if with_rms else 1
    naff = 2 if with_rms else 1

    with (
        nc.sbuf_tensor("xin", [128, 2 * D], F32) as xin,
        nc.sbuf_tensor("xbin", [128, (2 * D) if with_rms else 1], F32) as xbin,
        nc.sbuf_tensor("xnt", [128, 2 * D], BF16) as xnt,
        nc.sbuf_tensor("ht", [128, (2 * D) if with_rms else 1], BF16) as ht,
        nc.sbuf_tensor("stats", [128, 2 * 48], F32) as stats,
        nc.sbuf_tensor("mv", [128, 2 * 2], F32) as mv,
        nc.sbuf_tensor("tiny", [128, 16], F32) as tiny,
        nc.sbuf_tensor("cst", [128, 2], F32) as cst,
        nc.semaphore("s_fen") as s_fen,
        nc.semaphore("s_ld0") as s_ld0,
        nc.semaphore("s_ld1") as s_ld1,
        nc.semaphore("s_bn") as s_bn,
        nc.semaphore("s_rs") as s_rs,
        nc.semaphore("s_aff") as s_aff,
        nc.semaphore("s_st0") as s_st0,
        nc.semaphore("s_st1") as s_st1,
        nc.Block() as block,
    ):
        s_ld = [s_ld0, s_ld1]
        s_st = [s_st0, s_st1]

        @block.sync
        def _(sync):
            for i in range(ntiles):
                p = i % 2
                if i >= 2:
                    sync.wait_ge(s_aff, naff * (i - 1))
                    if with_rms:
                        sync.wait_ge(s_st[p], 16 * nst * (i // 2))
                sync.dma_start(out=xin[:, p * D:(p + 1) * D],
                               in_=xa[i * 128:(i + 1) * 128, :]).then_inc(s_ld[p], 16)
                if with_rms:
                    sync.dma_start(out=xbin[:, p * D:(p + 1) * D],
                                   in_=xb[i * 128:(i + 1) * 128, :]).then_inc(s_ld[p], 16)

        @block.vector
        def _(v):
            nfen = [0]
            v.memset(cst[:, 0:1], EPS)
            v.memset(cst[:, 1:2], EPS * EPS).then_inc(s_fen, 1)
            nfen[0] += 1
            v.wait_ge(s_fen, nfen[0])
            for i in range(ntiles):
                p = i % 2
                q = p * 8
                v.wait_ge(s_ld[p], 16 * nld * (i // 2 + 1))
                if i >= 2:
                    v.wait_ge(s_rs, i - 1)
                if with_rms:
                    v.tensor_add(xin[:, p * D:(p + 1) * D], xin[:, p * D:(p + 1) * D],
                                 xbin[:, p * D:(p + 1) * D]).then_inc(s_fen, 1)
                    nfen[0] += 1
                    v.wait_ge(s_fen, nfen[0])
                for j in range(8):
                    r = v.bn_stats(stats[:, p * 48 + j * 6: p * 48 + (j + 1) * 6],
                                   xin[:, p * D + j * 512: p * D + (j + 1) * 512])
                r.then_inc(s_fen, 1)
                nfen[0] += 1
                v.wait_ge(s_fen, nfen[0])
                v.bn_aggr(mv[:, p * 2:(p + 1) * 2], stats[:, p * 48:(p + 1) * 48]).then_inc(s_bn, 1)
                v.wait_ge(s_rs, i + 1)
                if i >= 2:
                    v.wait_ge(s_st[p], 16 * nst * (i // 2))
                r = v.tensor_scalar(xnt[:, p * D:(p + 1) * D], xin[:, p * D:(p + 1) * D],
                                    tiny[:, q + 2:q + 3], tiny[:, q + 0:q + 1],
                                    ALU.add, ALU.mult)
                if with_rms:
                    r = v.tensor_scalar(ht[:, p * D:(p + 1) * D], xin[:, p * D:(p + 1) * D],
                                        tiny[:, q + 2:q + 3], tiny[:, q + 1:q + 2],
                                        ALU.add, ALU.mult)
                r.then_inc(s_aff, naff)

        @block.scalar
        def _(s):
            for i in range(ntiles):
                p = i % 2
                q = p * 8
                s.wait_ge(s_bn, i + 1)
                s.activation(tiny[:, q + 5:q + 6], mv[:, p * 2 + 1: p * 2 + 2], AF.Ln, bias=cst[:, 0:1])
                s.activation(tiny[:, q + 0:q + 1], tiny[:, q + 5:q + 6], AF.Exp, scale=-0.5)
                s.activation(tiny[:, q + 2:q + 3], mv[:, p * 2: p * 2 + 1], AF.Identity, scale=-1.0)
                if with_rms:
                    s.activation(tiny[:, q + 5:q + 6], mv[:, p * 2 + 1: p * 2 + 2],
                                 AF.Ln, scale=1.0 + EPS, bias=cst[:, 1:2])
                    s.activation(tiny[:, q + 1:q + 2], tiny[:, q + 5:q + 6], AF.Exp, scale=-0.5)
                s.copy(tiny[:, q + 7:q + 8], tiny[:, q + 0:q + 1]).then_inc(s_rs, 1)

        @block.gpsimd
        def _(g):
            for i in range(ntiles):
                p = i % 2
                g.wait_ge(s_aff, naff * (i + 1))
                g.dma_start(out=xn[i * 128:(i + 1) * 128, :],
                            in_=xnt[:, p * D:(p + 1) * D]).then_inc(s_st[p], 16)
                if with_rms:
                    g.dma_start(out=h[i * 128:(i + 1) * 128, :],
                                in_=ht[:, p * D:(p + 1) * D]).then_inc(s_st[p], 16)
                    g.dma_start(out=xs[i * 128:(i + 1) * 128, :],
                                in_=xin[:, p * D:(p + 1) * D]).then_inc(s_st[p], 16)
            g.wait_ge(s_st0, 16 * nst * ((ntiles + 1) // 2))
            g.wait_ge(s_st1, 16 * nst * (ntiles // 2))
    return nc



def build_l0_launch(T: int = TC, debug=False):
    assert T % 512 == 0
    TB = T // 512
    nc = bass.Bass("TRN2", target_bir_lowering=False)
    if debug:
        dbm = nc.dram_tensor("dbm", [128, 128], F32, kind="ExternalOutput").ap()
        dws = nc.dram_tensor("dws", [32, 128, D], BF16, kind="ExternalOutput").ap()
    xn0f = nc.dram_tensor("xn0f", [D, T], BF16, kind="ExternalInput").ap()
    Wnk = nc.dram_tensor("Wnk", [D, D], F32, kind="ExternalInput").ap()
    Qnk = nc.dram_tensor("Qnk", [D, D], I32, kind="ExternalInput").ap()
    maskA = nc.dram_tensor("maskA", [D, D], I32, kind="ExternalInput").ap()
    szr = nc.dram_tensor("szr", [128, 32 * 32 * 2], F32, kind="ExternalInput").ap()
    Ebm = nc.dram_tensor("Ebm", [128, 32 * 128], BF16, kind="ExternalInput").ap()
    Pbm = nc.dram_tensor("Pbm", [128, 32 * 128], BF16, kind="ExternalInput").ap()
    identb = nc.dram_tensor("identb", [128, 128], BF16, kind="ExternalInput").ap()
    o0f = nc.dram_tensor("o0f", [D, T], F32, kind="ExternalOutput").ap()

    with ExitStack() as ctx:
        xsb = ctx.enter_context(nc.sbuf_tensor("xsb", [128, 32 * T], BF16))
        wbf = ctx.enter_context(nc.sbuf_tensor("wbf", [128, 2 * D], BF16))
        qbf = ctx.enter_context(nc.sbuf_tensor("qbf", [128, 2 * D], BF16))
        deq = ctx.enter_context(nc.sbuf_tensor("deq", [128, D], BF16))
        wsum = ctx.enter_context(nc.sbuf_tensor("wsum", [128, D], BF16))
        wT = ctx.enter_context(nc.sbuf_tensor("wT", [128, 2 * D], BF16))
        szc = ctx.enter_context(nc.sbuf_tensor("szc", [128, 2 * 64], F32))
        zpc = ctx.enter_context(nc.sbuf_tensor("zpc", [128, 2 * 32], F32))
        esb = ctx.enter_context(nc.sbuf_tensor("esb", [128, 2 * 128], BF16))
        idb = ctx.enter_context(nc.sbuf_tensor("idb", [128, 128], BF16))
        mred = ctx.enter_context(nc.sbuf_tensor("mred", [128, 2 * 128], BF16))
        bm = ctx.enter_context(nc.sbuf_tensor("bm", [128, 128], BF16))
        psl = ctx.enter_context(nc.sbuf_tensor("psl", [128, 128], BF16))
        osb = ctx.enter_context(nc.sbuf_tensor("osb", [128, T], F32))
        psA = ctx.enter_context(nc.psum_tensor("psA", [128, 512], F32))
        psB = ctx.enter_context(nc.psum_tensor("psB", [128, 512], F32))
        psC = ctx.enter_context(nc.psum_tensor("psC", [128, 512], F32))
        psD = ctx.enter_context(nc.psum_tensor("psD", [128, 512], F32))
        psM = ctx.enter_context(nc.psum_tensor("psM", [128, 512], F32))
        psT0 = ctx.enter_context(nc.psum_tensor("psT0", [128, 1024], BF16))
        psT1 = ctx.enter_context(nc.psum_tensor("psT1", [128, 1024], BF16))
        s_cld = ctx.enter_context(nc.semaphore("s_cld"))
        s_fen = ctx.enter_context(nc.semaphore("s_fen"))
        s_mld0 = ctx.enter_context(nc.semaphore("s_mld0"))
        s_mld1 = ctx.enter_context(nc.semaphore("s_mld1"))
        s_mrd = ctx.enter_context(nc.semaphore("s_mrd"))
        s_msum = ctx.enter_context(nc.semaphore("s_msum"))
        s_bm = ctx.enter_context(nc.semaphore("s_bm"))
        s_pld = ctx.enter_context(nc.semaphore("s_pld"))
        s_eld0 = ctx.enter_context(nc.semaphore("s_eld0"))
        s_eld1 = ctx.enter_context(nc.semaphore("s_eld1"))
        s_szl0 = ctx.enter_context(nc.semaphore("s_szl0"))
        s_szl1 = ctx.enter_context(nc.semaphore("s_szl1"))
        s_wld0 = ctx.enter_context(nc.semaphore("s_wld0"))
        s_wld1 = ctx.enter_context(nc.semaphore("s_wld1"))
        s_mx = ctx.enter_context(nc.semaphore("s_mx"))
        s_wsum = ctx.enter_context(nc.semaphore("s_wsum"))
        s_tp = ctx.enter_context(nc.semaphore("s_tp"))
        s_tx = ctx.enter_context(nc.semaphore("s_tx"))
        s_mm = ctx.enter_context(nc.semaphore("s_mm"))
        s_ev = ctx.enter_context(nc.semaphore("s_ev"))
        s_st = ctx.enter_context(nc.semaphore("s_st"))
        if debug:
            dbmb = ctx.enter_context(nc.sbuf_tensor("dbmb", [128, 128], F32))
            s_dbw = ctx.enter_context(nc.semaphore("s_dbw"))
            s_dbg = ctx.enter_context(nc.semaphore("s_dbg"))
        block = ctx.enter_context(nc.Block())
        psT = [psT0, psT1]
        psMM = [psA, psB, psC, psD]
        s_mld = [s_mld0, s_mld1]
        s_eld = [s_eld0, s_eld1]
        s_szl = [s_szl0, s_szl1]
        s_wld = [s_wld0, s_wld1]

        @block.sync
        def _(sync):
            sync.dma_start(out=idb[:], in_=identb[:]).then_inc(s_cld, 16)
            for kc in range(32):
                sync.dma_start(out=xsb[:, kc * T:(kc + 1) * T],
                               in_=xn0f[kc * 128:(kc + 1) * 128, :]).then_inc(s_cld, 16)
            for t in range(32):
                pe = t % 2
                if t >= 2:
                    sync.wait_ge(s_msum, t - 1)
                sync.dma_start(out=esb[:, pe * 128:(pe + 1) * 128],
                               in_=Ebm[:, t * 128:(t + 1) * 128]).then_inc(s_eld[pe], 16)
            if debug:
                sync.wait_ge(s_dbw, 1)
                sync.dma_start(out=dbm[:], in_=dbmb[:]).then_inc(s_st, 16)
            for ncn in range(32):
                if debug:
                    sync.wait_ge(s_wsum, ncn + 1)
                    sync.dma_start(out=dws[ncn], in_=wsum[:]).then_inc(s_dbg, 16)
                sync.wait_ge(s_ev, ncn + 1)
                sync.dma_start(out=o0f[ncn * 128:(ncn + 1) * 128, :],
                               in_=osb[:]).then_inc(s_st, 16)
            sync.wait_ge(s_st, 16 * (32 + 1 if debug else 32))

        @block.gpsimd
        def _(g):
            for t in range(32):
                p = t % 2
                if t >= 2:
                    g.wait_ge(s_mrd, t - 1)
                g.dma_start(out=wbf[:, p * D:(p + 1) * D],
                            in_=maskA[t * 128:(t + 1) * 128, :]).then_inc(s_mld[p], 16)
            for ncn in range(32):
                p = ncn % 2
                if ncn >= 1:
                    g.wait_ge(s_mx, ncn)
                g.dma_start(out=psl[:], in_=Pbm[:, ncn * 128:(ncn + 1) * 128]).then_inc(s_pld, 16)
                if ncn >= 2:
                    g.wait_ge(s_wsum, ncn - 1)
                g.dma_start(out=szc[:, p * 64:(p + 1) * 64],
                            in_=szr[:, ncn * 64:(ncn + 1) * 64]).then_inc(s_szl[p], 16)
                if ncn == 0:
                    g.wait_ge(s_bm, 1)
                if ncn >= 2:
                    g.wait_ge(s_wsum, ncn - 1)
                g.dma_start(out=wbf[:, p * D:(p + 1) * D],
                            in_=Wnk[ncn * 128:(ncn + 1) * 128, :]).then_inc(s_wld[p], 16)
                g.dma_start(out=qbf[:, p * D:(p + 1) * D],
                            in_=Qnk[ncn * 128:(ncn + 1) * 128, :]).then_inc(s_wld[p], 16)

        @block.vector
        def _(v):
            nfen = [0]

            def fence(r):
                r.then_inc(s_fen, 1)
                nfen[0] += 1
                v.wait_ge(s_fen, nfen[0])
            for t in range(32):
                p = t % 2
                v.wait_ge(s_mld[p], 16 * (t // 2 + 1))
                if t >= 2:
                    v.wait_ge(s_msum, t - 1)
                with nc.allow_low_precision(reason="mask 0/1 sums<=32 exact in bf16"):
                    v.reduce_sum(mred[:, p * 128:(p + 1) * 128],
                                 wbf[:, p * D:(p + 1) * D].rearrange("p (a b) -> p a b", b=32),
                                 axis=mybir.AxisListType.X).then_inc(s_mrd, 1)
            v.wait_ge(s_msum, 32)
            v.tensor_scalar(bm[:], psM[:, 0:128], 0.0, None, ALU.is_gt).then_inc(s_bm, 1)
            if debug:
                v.tensor_copy(dbmb[:], bm[:]).then_inc(s_dbw, 1)
            for ncn in range(32):
                p = ncn % 2
                v.wait_ge(s_wld[p], 32 * (ncn // 2 + 1))
                v.wait_ge(s_szl[p], 16 * (ncn // 2 + 1))
                szc3 = szc[:, p * 64:(p + 1) * 64].rearrange("p (g c) -> p g c", c=2)
                fence(v.scalar_tensor_tensor(zpc[:, p * 32:(p + 1) * 32], szc3[:, :, 0], -8.0,
                                             szc3[:, :, 1], ALU.mult, ALU.add))
                for gix in range(32):
                    v.tensor_scalar(deq[:, gix * 128:(gix + 1) * 128],
                                    qbf[:, p * D + gix * 128: p * D + (gix + 1) * 128],
                                    szc3[:, gix:gix + 1, 0],
                                    zpc[:, p * 32 + gix: p * 32 + gix + 1],
                                    ALU.mult, ALU.add)
                v.wait_ge(s_mx, ncn + 1)
                wmv = qbf[:, p * D:(p + 1) * D]
                fence(v.tensor_tensor(wmv.rearrange("p (a b) -> p a b", b=32),
                                      wbf[:, p * D:(p + 1) * D].rearrange("p (a b) -> p a b", b=32),
                                      psM[:, 0:128].unsqueeze(2).broadcast_to([128, 128, 32]),
                                      ALU.mult))
                if debug and ncn >= 1:
                    v.wait_ge(s_dbg, 16 * ncn)
                v.tensor_tensor(wsum[:], wmv, deq[:], ALU.add).then_inc(s_wsum, 1)
                v.wait_ge(s_mm, ncn + 1)
                if ncn >= 1:
                    v.wait_ge(s_st, 16 * ncn)
                for tb in range(TB):
                    r = v.tensor_copy(osb[:, tb * 512:(tb + 1) * 512], psMM[tb % 4][:])
                r.then_inc(s_ev, 1)

        @block.scalar
        def _(s):
            for ncn in range(32):
                pw = ncn % 2
                for kc in range(32):
                    s.wait_ge(s_tp, ncn * 32 + kc + 1)
                    r = s.copy(wT[:, pw * D + kc * 128: pw * D + (kc + 1) * 128],
                               psT[kc % 2][:, 0:128])
                    r.then_inc(s_tx, 1)

        @block.tensor
        def _(t):
            pass
            for tt in range(32):
                t.wait_ge(s_mrd, tt + 1)
                t.wait_ge(s_eld[tt % 2], 16 * (tt // 2 + 1))
                t.matmul(psM[:, 0:128], esb[:, (tt % 2) * 128:(tt % 2 + 1) * 128],
                         mred[:, (tt % 2) * 128:(tt % 2 + 1) * 128],
                         start=(tt == 0), stop=(tt == 31)).then_inc(s_msum, 1)
            t.wait_ge(s_cld, 16 * 33)
            for ncn in range(32):
                pw = ncn % 2
                t.wait_ge(s_bm, 1)
                t.wait_ge(s_pld, 16 * (ncn + 1))
                if ncn >= 1:
                    t.wait_ge(s_wsum, ncn)
                t.matmul(psM[:, 0:128], psl[:], bm[:], start=True, stop=True).then_inc(s_mx, 1)
                t.wait_ge(s_wsum, ncn + 1)
                for kc in range(32):
                    if ncn * 32 + kc >= 2:
                        t.wait_ge(s_tx, ncn * 32 + kc - 1)
                    t.transpose(psT[kc % 2][:, 0:128],
                                wsum[:, kc * 128:(kc + 1) * 128], idb[:]).then_inc(s_tp, 1)
                t.wait_ge(s_tx, 32 * (ncn + 1))
                if ncn >= 1:
                    t.wait_ge(s_ev, ncn)
                for kc in range(32):
                    for tb in range(TB):
                        r = t.matmul(psMM[tb % 4][:],
                                     wT[:, pw * D + kc * 128: pw * D + (kc + 1) * 128],
                                     xsb[:, kc * T + tb * 512: kc * T + (tb + 1) * 512],
                                     start=(kc == 0), stop=(kc == 31))
                r.then_inc(s_mm, 1)
    return nc



def _t(a):
    return np.ascontiguousarray(a.T)


def _make_l0_consts():
    E = np.zeros((32, 128, 128), dtype=BF16NP)
    P = np.zeros((32, 128, 128), dtype=BF16NP)
    for t in range(32):
        for n in range(128):
            E[t, n, 4 * t + n // 32] = 1
            P[t, 4 * t + n // 32, n] = 1
    Ebm = np.ascontiguousarray(E.transpose(1, 0, 2).reshape(128, 32 * 128))
    Pbm = np.ascontiguousarray(P.transpose(1, 0, 2).reshape(128, 32 * 128))
    identb = np.eye(128, dtype=BF16NP)
    return Ebm, Pbm, identb


def _stage_szr(scales_and_zeros):
    sz = np.asarray(scales_and_zeros, dtype=np.float32)
    out = sz.transpose(1, 0, 2).reshape(32, 128, 32, 2).transpose(1, 0, 2, 3)
    return np.ascontiguousarray(out.reshape(128, 32 * 32 * 2))


def run_ln0(x0_shards):
    nc = build_ln_launch(with_rms=False)
    in_maps = [{"xa": s} for s in x0_shards]
    res = run_bass_kernel_spmd(nc, in_maps, core_ids=list(range(NCORES)))
    return [r["xn"] for r in res.results]


def run_l0(xn0f_shards, Wnk, Qnk, maskA, szr, Ebm, Pbm, identb):
    nc = build_l0_launch()
    in_maps = [{"xn0f": s, "Wnk": Wnk, "Qnk": Qnk, "maskA": maskA, "szr": szr,
                "Ebm": Ebm, "Pbm": Pbm, "identb": identb} for s in xn0f_shards]
    res = run_bass_kernel_spmd(nc, in_maps, core_ids=list(range(NCORES)))
    return [r["o0f"] for r in res.results]


def run_ln1(x0_shards, o0_shards):
    nc = build_ln_launch(with_rms=True)
    in_maps = [{"xa": a, "xb": b} for a, b in zip(x0_shards, o0_shards)]
    res = run_bass_kernel_spmd(nc, in_maps, core_ids=list(range(NCORES)))
    return ([r["xs"] for r in res.results], [r["xn"] for r in res.results],
            [r["h"] for r in res.results])






def build_proj_launch(T: int = TC, TH: int = 2176):
    NB = 145
    nc = bass.Bass("TRN2", target_bir_lowering=False)
    hf = nc.dram_tensor("hf", [D, TH], BF16, kind="ExternalInput").ap()
    WinT = nc.dram_tensor("WinT", [128, NB, 32, 128], F32, kind="ExternalInput").ap()
    nwf = nc.dram_tensor("nwf", [128, 32], F32, kind="ExternalInput").ap()
    convw = nc.dram_tensor("convw", [128, 80 * 4], F32, kind="ExternalInput").ap()
    convb = nc.dram_tensor("convb", [128, 80], F32, kind="ExternalInput").ap()
    dtb = nc.dram_tensor("dtb", [128, 1], F32, kind="ExternalInput").ap()
    sg = nc.dram_tensor("sg", [64, 128, T], BF16, kind="ExternalOutput").ap()
    hid = nc.dram_tensor("hid", [64, 128, T], F32, kind="ExternalOutput").ap()
    Bc = nc.dram_tensor("Bc", [8, 128, T], F32, kind="ExternalOutput").ap()
    Cc = nc.dram_tensor("Cc", [8, 128, T], F32, kind="ExternalOutput").ap()
    dto = nc.dram_tensor("dto", [128, T], F32, kind="ExternalOutput").ap()

    MAIN = [(1, 128, 512), (2, 640, 512), (3, 1152, 512), (4, 1664, 512)]
    HALO = (0, 0, 128)

    with ExitStack() as ctx:
        hsb = ctx.enter_context(nc.sbuf_tensor("hsb", [128, 32 * TH], BF16))
        wsb = ctx.enter_context(nc.sbuf_tensor("wsb", [128, 2 * 32 * 128], BF16))
        tmpf = ctx.enter_context(nc.sbuf_tensor("tmpf", [128, TH], BF16))
        cvin = ctx.enter_context(nc.sbuf_tensor("cvin", [128, TH], F32))
        acc0 = ctx.enter_context(nc.sbuf_tensor("acc0", [128, T], F32))
        acc1 = ctx.enter_context(nc.sbuf_tensor("acc1", [128, T], F32))
        sgb = ctx.enter_context(nc.sbuf_tensor("sgb", [128, T], BF16))
        hcb = ctx.enter_context(nc.sbuf_tensor("hcb", [128, T], F32))
        dts = ctx.enter_context(nc.sbuf_tensor("dts", [128, T], F32))
        nws = ctx.enter_context(nc.sbuf_tensor("nws", [128, 32], F32))
        cws = ctx.enter_context(nc.sbuf_tensor("cws", [128, 80 * 4], F32))
        cbs = ctx.enter_context(nc.sbuf_tensor("cbs", [128, 80], F32))
        dtbs = ctx.enter_context(nc.sbuf_tensor("dtbs", [128, 1], F32))
        scs = ctx.enter_context(nc.sbuf_tensor("scs", [128, 2], F32))
        vcs = ctx.enter_context(nc.sbuf_tensor("vcs", [128, 2], F32))
        psH = ctx.enter_context(nc.psum_tensor("psH", [128, 512], F32))
        ps1 = ctx.enter_context(nc.psum_tensor("ps1", [128, 512], F32))
        ps2 = ctx.enter_context(nc.psum_tensor("ps2", [128, 512], F32))
        ps3 = ctx.enter_context(nc.psum_tensor("ps3", [128, 512], F32))
        ps4 = ctx.enter_context(nc.psum_tensor("ps4", [128, 512], F32))
        s_cld = ctx.enter_context(nc.semaphore("s_cld"))
        s_h = ctx.enter_context(nc.semaphore("s_h"))
        s_fold = ctx.enter_context(nc.semaphore("s_fold"))
        s_w0 = ctx.enter_context(nc.semaphore("s_w0"))
        s_w1 = ctx.enter_context(nc.semaphore("s_w1"))
        s_mm = ctx.enter_context(nc.semaphore("s_mm"))
        s_ev = ctx.enter_context(nc.semaphore("s_ev"))
        s_cv = ctx.enter_context(nc.semaphore("s_cv"))
        s_sp = ctx.enter_context(nc.semaphore("s_sp"))
        s_sl = ctx.enter_context(nc.semaphore("s_sl"))
        s_fen = ctx.enter_context(nc.semaphore("s_fen"))
        s_fs = ctx.enter_context(nc.semaphore("s_fs"))
        s_sg = ctx.enter_context(nc.semaphore("s_sg"))
        s_hc = ctx.enter_context(nc.semaphore("s_hc"))
        s_dt = ctx.enter_context(nc.semaphore("s_dt"))
        s_stg = ctx.enter_context(nc.semaphore("s_stg"))
        s_sth = ctx.enter_context(nc.semaphore("s_sth"))
        s_std = ctx.enter_context(nc.semaphore("s_std"))
        block = ctx.enter_context(nc.Block())
        psMAIN = [None, ps1, ps2, ps3, ps4]
        s_w = [s_w0, s_w1]
        dtv = cvin

        def role(nb):
            if nb < 64:
                return "gate"
            if nb < 144:
                return "hbc"
            return "dt"

        @block.sync
        def _(sync):
            sync.dma_start(out=nws[:], in_=nwf[:]).then_inc(s_cld, 16)
            sync.dma_start(out=cws[:], in_=convw[:]).then_inc(s_cld, 16)
            sync.dma_start(out=cbs[:], in_=convb[:]).then_inc(s_cld, 16)
            sync.dma_start(out=dtbs[:], in_=dtb[:]).then_inc(s_cld, 16)
            for kc in range(32):
                sync.dma_start(out=hsb[:, kc * TH:(kc + 1) * TH],
                               in_=hf[kc * 128:(kc + 1) * 128, :]).then_inc(s_h, 16)
            isg = 0; ihc = 0
            for nb in range(NB):
                r = role(nb)
                if r == "gate":
                    isg += 1
                    sync.wait_ge(s_sg, isg)
                    sync.dma_start(out=sg[nb], in_=sgb[:]).then_inc(s_stg, 16)
                elif r == "hbc":
                    cc = nb - 64
                    ihc += 1
                    sync.wait_ge(s_hc, ihc)
                    dst = hid[cc] if cc < 64 else (Bc[cc - 64] if cc < 72 else Cc[cc - 72])
                    sync.dma_start(out=dst, in_=hcb[:]).then_inc(s_sth, 16)
                else:
                    sync.wait_ge(s_dt, 1)
                    sync.dma_start(out=dto[:], in_=cvin[:, 0:T]).then_inc(s_std, 16)
            sync.wait_ge(s_stg, 16 * 64)
            sync.wait_ge(s_sth, 16 * 80)
            sync.wait_ge(s_std, 16)

        @block.gpsimd
        def _(g):
            for nb in range(NB):
                p = nb % 2
                if nb >= 2:
                    g.wait_ge(s_mm, nb - 1)
                g.dma_start(out=wsb[:, p * 4096:(p + 1) * 4096].rearrange("p (a b) -> p a b", b=128),
                            in_=WinT[:, nb]).then_inc(s_w[p], 16)

        @block.vector
        def _(v):
            nfen = [0]

            def fence(r):
                r.then_inc(s_fen, 1)
                nfen[0] += 1
                v.wait_ge(s_fen, nfen[0])

            def fence_new():
                fence(v.tensor_copy(vcs[:, 0:1], vcs[:, 1:2]))

            v.wait_ge(s_h, 16 * 32)
            v.wait_ge(s_cld, 16 * 4)
            for kc in range(32):
                fence(v.tensor_scalar_mul(tmpf[:], hsb[:, kc * TH:(kc + 1) * TH],
                                          nws[:, kc:kc + 1]))
                fence(v.tensor_copy(hsb[:, kc * TH:(kc + 1) * TH], tmpf[:]))
            for nb in range(NB):
                r = role(nb)
                if r == "gate":
                    continue
                v.wait_ge(s_mm, nb + 1)
                if r == "hbc":
                    cc = nb - 64
                    v.tensor_copy(cvin[:, 0:128], psH[:, 0:128])
                    for (bk, off, sz) in MAIN:
                        rr = v.tensor_copy(cvin[:, off:off + sz], psMAIN[bk][:])
                    rr.then_inc(s_ev, 1)
                    fence_new()
                    if cc >= 1:
                        v.wait_ge(s_hc, cc)
                    fence(v.tensor_scalar_mul(acc0[:], cvin[:, 125:125 + T],
                                              cws[:, cc * 4:cc * 4 + 1]))
                    fence(v.scalar_tensor_tensor(acc1[:], cvin[:, 126:126 + T],
                                                 cws[:, cc * 4 + 1:cc * 4 + 2], acc0[:],
                                                 ALU.mult, ALU.add))
                    fence(v.scalar_tensor_tensor(acc0[:], cvin[:, 127:127 + T],
                                                 cws[:, cc * 4 + 2:cc * 4 + 3], acc1[:],
                                                 ALU.mult, ALU.add))
                    v.scalar_tensor_tensor(acc1[:], cvin[:, 128:128 + T],
                                           cws[:, cc * 4 + 3:cc * 4 + 4], acc0[:],
                                           ALU.mult, ALU.add).then_inc(s_cv, 1)
                else:
                    for (bk, off, sz) in MAIN:
                        j = (off - 128)
                        rr = v.tensor_scalar_add(dtv[:, j:j + sz], psMAIN[bk][:], dtbs[:])
                    rr.then_inc(s_ev, 1)
                    fence_new()
                    v.tensor_copy(vcs[:, 0:1], vcs[:, 1:2]).then_inc(s_cv, 1)
                    v.wait_ge(s_sp, 1)
                    fence(v.tensor_relu(dts[:], dtv[:, 0:T]))
                    v.tensor_tensor(dtv[:, 0:T], dts[:], acc0[:], ALU.add).then_inc(s_dt, 1)

        @block.scalar
        def _(s):
            nfs = [0]

            def fens(r):
                r.then_inc(s_fs, 1)
                nfs[0] += 1
                s.wait_ge(s_fs, nfs[0])

            isg = 0; icv = 0
            for nb in range(NB):
                r = role(nb)
                if r == "gate":
                    s.wait_ge(s_mm, nb + 1)
                    if isg >= 1:
                        s.wait_ge(s_stg, 16 * isg)
                    for (bk, off, sz) in MAIN:
                        j = off - 128
                        rr = s.activation(sgb[:, j:j + sz], psMAIN[bk][:], AF.Silu)
                    rr.then_inc(s_ev, 1)
                    s.copy(scs[:, 0:1], scs[:, 1:2]).then_inc(s_sg, 1)
                    isg += 1
                elif r == "hbc":
                    cc = nb - 64
                    icv += 1
                    s.wait_ge(s_cv, icv)
                    if cc >= 1:
                        s.wait_ge(s_sth, 16 * cc)
                    for j in range(0, T, 512):
                        rr = s.activation(hcb[:, j:j + 512], acc1[:, j:j + 512], AF.Silu,
                                          bias=cbs[:, cc:cc + 1])
                    s.copy(scs[:, 0:1], scs[:, 1:2]).then_inc(s_hc, 1)
                else:
                    s.wait_ge(s_cv, icv + 1)
                    for j in range(0, T, 512):
                        rr = s.activation(dts[:, j:j + 512], cvin[:, j:j + 512], AF.Abs)
                    fens(rr)
                    for j in range(0, T, 512):
                        rr = s.activation(acc1[:, j:j + 512], dts[:, j:j + 512], AF.Exp, scale=-1.0)
                    fens(rr)
                    for j in range(0, T, 512):
                        s.activation(acc0[:, j:j + 512], acc1[:, j:j + 512], AF.Ln, bias=1.0)
                    s.copy(scs[:, 0:1], scs[:, 1:2]).then_inc(s_sp, 1)

        @block.tensor
        def _(t):
            t.wait_ge(s_fen, 64)
            for nb in range(NB):
                p = nb % 2
                r = role(nb)
                t.wait_ge(s_w[p], 16 * (nb // 2 + 1))
                if nb >= 1:
                    t.wait_ge(s_ev, nb)
                blocks = ([HALO] + MAIN) if r == "hbc" else MAIN
                for kc in range(32):
                    lhsT = wsb[:, p * 4096 + kc * 128: p * 4096 + (kc + 1) * 128]
                    for (bk, off, sz) in blocks:
                        ps = psH if bk == 0 else psMAIN[bk]
                        rr = t.matmul(ps[:, 0:sz], lhsT,
                                      hsb[:, kc * TH + off: kc * TH + off + sz],
                                      start=(kc == 0), stop=(kc == 31))
                rr.then_inc(s_mm, 1)
    return nc


def _stage_proj_w(w):
    W4 = np.asarray(w, np.float32).reshape(145, 128, 32, 128)
    return np.ascontiguousarray(W4.transpose(3, 0, 2, 1))


def _stage_conv(conv_w, conv_b, norm_w, dt_bias):
    cw = np.ascontiguousarray(np.asarray(conv_w, np.float32).reshape(80, 128, 4)
                              .transpose(1, 0, 2).reshape(128, 320))
    cb = np.ascontiguousarray(np.asarray(conv_b, np.float32).reshape(80, 128).T)
    nw = np.ascontiguousarray(np.asarray(norm_w, np.float32).reshape(32, 128).T)
    db = np.ascontiguousarray(np.asarray(dt_bias, np.float32).reshape(128, 1))
    return cw, cb, nw, db


def run_proj(hf_shards, WinT, nwf, convw, convb, dtb):
    nc = build_proj_launch()
    in_maps = [{"hf": s, "WinT": WinT, "nwf": nwf, "convw": convw,
                "convb": convb, "dtb": dtb} for s in hf_shards]
    res = run_bass_kernel_spmd(nc, in_maps, core_ids=list(range(len(hf_shards))))
    return res.results



def build_ssd_launch(NU: int = 4, NCH: int = 32):
    SQ = NCH * 128
    nc = bass.Bass("TRN2", target_bir_lowering=False)
    hid4 = nc.dram_tensor("hid4", [NU, 8, 128, SQ], F32, kind="ExternalInput").ap()
    sg4 = nc.dram_tensor("sg4", [NU, 8, 128, SQ], BF16, kind="ExternalInput").ap()
    B4 = nc.dram_tensor("B4", [NU, 128, SQ], F32, kind="ExternalInput").ap()
    C4 = nc.dram_tensor("C4", [NU, 128, SQ], F32, kind="ExternalInput").ap()
    dt4 = nc.dram_tensor("dt4", [NU, 16, SQ], F32, kind="ExternalInput").ap()
    alog = nc.dram_tensor("alog", [NU, 16, 1], F32, kind="ExternalInput").ap()
    dpb = nc.dram_tensor("dpb", [128, NU * 16], F32, kind="ExternalInput").ap()
    idf = nc.dram_tensor("idf", [128, 128], F32, kind="ExternalInput").ap()
    idbm = nc.dram_tensor("idbm", [128, 128], BF16, kind="ExternalInput").ap()
    triu = nc.dram_tensor("triu", [128, 128], F32, kind="ExternalInput").ap()
    ones16 = nc.dram_tensor("ones16", [16, 128], F32, kind="ExternalInput").ap()
    i16 = nc.dram_tensor("i16", [16, 16], F32, kind="ExternalInput").ap()
    idb64 = nc.dram_tensor("idb64", [128, 64], BF16, kind="ExternalInput").ap()
    negrow = nc.dram_tensor("negrow", [2, 2048], F32, kind="ExternalInput").ap()
    y4 = nc.dram_tensor("y4", [NU, NCH, 128, 1024], BF16, kind="ExternalOutput").ap()

    NC_TOT = NU * NCH

    def ne_of(u, c):
        return u * (NCH - 1) + c

    with ExitStack() as ctx:
        hidc = ctx.enter_context(nc.sbuf_tensor("hidc", [128, 2 * 1024], BF16))
        sgc = ctx.enter_context(nc.sbuf_tensor("sgc", [128, 2 * 1024], BF16))
        sgt = ctx.enter_context(nc.sbuf_tensor("sgt", [128, 1024], BF16))
        Bb = ctx.enter_context(nc.sbuf_tensor("Bb", [128, SQ], BF16))
        Cb = ctx.enter_context(nc.sbuf_tensor("Cb", [128, SQ], BF16))
        dtu = ctx.enter_context(nc.sbuf_tensor("dtu", [16, SQ], F32))
        alg = ctx.enter_context(nc.sbuf_tensor("alg", [16, 1], F32))
        nega = ctx.enter_context(nc.sbuf_tensor("nega", [16, 1], F32))
        acv = ctx.enter_context(nc.sbuf_tensor("acv", [16, 128], F32))
        acs = ctx.enter_context(nc.sbuf_tensor("acs", [16, 128], F32))
        ons16 = ctx.enter_context(nc.sbuf_tensor("ons16", [16, 128], F32))
        i16s = ctx.enter_context(nc.sbuf_tensor("i16s", [16, 16], F32))
        diag = ctx.enter_context(nc.sbuf_tensor("diag", [16, 16], F32))
        ell = ctx.enter_context(nc.sbuf_tensor("ell", [16, 2], F32))
        dop = ctx.enter_context(nc.sbuf_tensor("dop", [2, 2 * 2048], F32))
        dtT = ctx.enter_context(nc.sbuf_tensor("dtT", [128, 16], F32))
        acsT = ctx.enter_context(nc.sbuf_tensor("acsT", [128, 16], F32))
        EA = ctx.enter_context(nc.sbuf_tensor("EA", [128, 16], F32))
        Ebc = ctx.enter_context(nc.sbuf_tensor("Ebc", [128, 16], F32))
        Msk = ctx.enter_context(nc.sbuf_tensor("Msk", [128, 128], F32))
        BTs = ctx.enter_context(nc.sbuf_tensor("BTs", [128, 128], BF16))
        LTm = ctx.enter_context(nc.sbuf_tensor("LTm", [128, 2 * 128], F32))
        LTe = ctx.enter_context(nc.sbuf_tensor("LTe", [128, 2 * 128], F32))
        Ylh = ctx.enter_context(nc.sbuf_tensor("Ylh", [128, 2 * 128], BF16))
        ddv = ctx.enter_context(nc.sbuf_tensor("ddv", [128, 2], F32))
        hdtp = ctx.enter_context(nc.sbuf_tensor("hdtp", [128, 1024], BF16))
        hcpp = ctx.enter_context(nc.sbuf_tensor("hcpp", [128, 1024], BF16))
        srhp = ctx.enter_context(nc.sbuf_tensor("srhp", [128, 1024], BF16))
        prevb = ctx.enter_context(nc.sbuf_tensor("prevb", [128, 2 * 1024], F32))
        prevbf = ctx.enter_context(nc.sbuf_tensor("prevbf", [128, 1024], BF16))
        stb = ctx.enter_context(nc.sbuf_tensor("stb", [128, 2 * 1024], F32))
        ysb = ctx.enter_context(nc.sbuf_tensor("ysb", [128, 1024], F32))
        y2b = ctx.enter_context(nc.sbuf_tensor("y2b", [128, 1024], F32))
        ygb = ctx.enter_context(nc.sbuf_tensor("ygb", [128, 1024], F32))
        jnk = ctx.enter_context(nc.sbuf_tensor("jnk", [128, 1024], F32))
        ssq = ctx.enter_context(nc.sbuf_tensor("ssq", [128, 1], F32))
        rgs = ctx.enter_context(nc.sbuf_tensor("rgs", [128, 2], F32))
        ybf = ctx.enter_context(nc.sbuf_tensor("ybf", [128, 2 * 1024], BF16))
        dps = ctx.enter_context(nc.sbuf_tensor("dps", [128, NU * 16], F32))
        idfs = ctx.enter_context(nc.sbuf_tensor("idfs", [128, 128], F32))
        idbs = ctx.enter_context(nc.sbuf_tensor("idbs", [128, 128], BF16))
        trius = ctx.enter_context(nc.sbuf_tensor("trius", [128, 128], F32))
        idb64s = ctx.enter_context(nc.sbuf_tensor("idb64s", [128, 64], BF16))
        epsc = ctx.enter_context(nc.sbuf_tensor("epsc", [128, 1], F32))
        vcs = ctx.enter_context(nc.sbuf_tensor("vcs", [128, 2], F32))
        scs = ctx.enter_context(nc.sbuf_tensor("scs", [128, 2], F32))
        psY0 = ctx.enter_context(nc.psum_tensor("psY0", [128, 512], F32))
        psY1 = ctx.enter_context(nc.psum_tensor("psY1", [128, 512], F32))
        psDa = ctx.enter_context(nc.psum_tensor("psDa", [128, 512], F32))
        psDb = ctx.enter_context(nc.psum_tensor("psDb", [128, 512], F32))
        psTa = ctx.enter_context(nc.psum_tensor("psTa", [128, 1024], BF16))
        psTb = ctx.enter_context(nc.psum_tensor("psTb", [128, 1024], BF16))
        psSa = ctx.enter_context(nc.psum_tensor("psSa", [128, 512], F32))
        psSb = ctx.enter_context(nc.psum_tensor("psSb", [128, 512], F32))
        sems = {}
        for nm in ["cld", "hl0", "hl1", "sl0", "sl1", "ucl", "acs", "el", "dgv",
                   "dtt", "act", "ebm", "ebe", "prv", "bt", "btv", "mt", "msk",
                   "sgte", "dmm", "ltm", "lte", "ylh", "ht", "hde", "yd",
                   "sm", "ste", "yo", "yev", "ya", "yg", "sq", "ybo", "yst0", "yst1",
                   "fen", "fes", "dml"]:
            sems[nm] = ctx.enter_context(nc.semaphore("s_" + nm))
        block = ctx.enter_context(nc.Block())
        S = sems
        s_hl = [S["hl0"], S["hl1"]]
        s_sl = [S["sl0"], S["sl1"]]
        psD = [psDa, psDb]
        psT_ = [psTa, psTb]
        psY = [psY0, psY1]
        psS_ = [psSa, psSb]
        hid3 = hidc[:].rearrange("p (b r t) -> p b r t", b=2, r=8)

        @block.sync
        def _(sync):
            for ap_, sb in [(idf, idfs), (idbm, idbs), (triu, trius), (ones16, ons16),
                            (i16, i16s), (dpb, dps), (idb64, idb64s)]:
                sync.dma_start(out=sb[:], in_=ap_[:]).then_inc(S["cld"], 16)
            for ci in range(NC_TOT):
                u, c = divmod(ci, NCH)
                pb = ci % 2
                sync.wait_ge(S["ybo"], ci + 1)
                sync.dma_start(out=y4[u, c], in_=ybf[:, pb * 1024:(pb + 1) * 1024]).then_inc(S["yst0" if pb == 0 else "yst1"], 16)
            sync.wait_ge(S["yst0"], 16 * ((NC_TOT + 1) // 2))
            sync.wait_ge(S["yst1"], 16 * (NC_TOT // 2))

        @block.gpsimd
        def _(g):
            for u in range(NU):
                if u >= 1:
                    g.wait_ge(S["ybo"], u * NCH)
                if u == 0:
                    g.dma_start(out=dop[0:2, 2048:4096], in_=negrow[:]).then_inc(S["dml"], 16)
                g.dma_start(out=Bb[:], in_=B4[u]).then_inc(S["ucl"], 16)
                g.dma_start(out=Cb[:], in_=C4[u]).then_inc(S["ucl"], 16)
                g.dma_start(out=dtu[:], in_=dt4[u]).then_inc(S["ucl"], 16)
                g.dma_start(out=alg[:], in_=alog[u]).then_inc(S["ucl"], 16)
                for c in range(NCH):
                    ci = u * NCH + c
                    p = ci % 2
                    if ci >= 2:
                        g.wait_ge(S["ht"], 16 * (ci - 1))
                        g.wait_ge(S["sgte"], ci - 1)
                    g.dma_start(out=hid3[:, p], in_=hid4[u, :, :, c * 128:(c + 1) * 128]
                                .rearrange("r p t -> p r t")).then_inc(s_hl[p], 16)
                    g.dma_start(out=sgc[:, p * 1024:(p + 1) * 1024].rearrange("p (r t) -> p r t", r=8),
                                in_=sg4[u, :, :, c * 128:(c + 1) * 128].rearrange("r p t -> p r t")
                                ).then_inc(s_sl[p], 16)
                    g.wait_ge(S["acs"], ci + 1)
                    if ci >= 1:
                        g.wait_ge(S["dmm"], 16 * ci)
                    g.dma_start(out=dop[1:2, 0:2048], in_=acs[:, :]).then_inc(S["dml"], 16)
                    g.dma_start(out=dop[0:1, 2048:4096], in_=acs[:, :]).then_inc(S["dml"], 16)

        @block.vector
        def _(v):
            nfen = [0]

            def fence(r):
                r.then_inc(S["fen"], 1)
                nfen[0] += 1
                v.wait_ge(S["fen"], nfen[0])

            v.wait_ge(S["cld"], 16 * 7)
            v.memset(epsc[:], EPS)
            v.memset(dop[0:1, 0:2048], 1.0)
            fence(v.memset(vcs[:], 0.0))
            for u in range(NU):
                v.wait_ge(S["ucl"], 16 * 4 * (u + 1))
                v.wait_ge(S["el"], u * (NCH + 1) + 1)
                fence(v.tensor_scalar_mul(nega[:], ell[:, 1:2], -1.0))
                v.memset(prevb[:, 0:1024], 0.0)
                fence(v.memset(prevb[:, 1024:2048], 0.0))
                for c in range(NCH):
                    ci = u * NCH + c
                    cp = c % 2
                    ne = ne_of(u, c)
                    if ci >= 1:
                        v.wait_ge(S["dml"], 16 + 32 * ci)
                        v.wait_ge(S["bt"], 2 * (ci - 1) + 1)
                        v.wait_ge(S["el"], u * (NCH + 1) + c + 1)
                    fence(v.tensor_scalar_mul(acv[:], dtu[:, c * 128:(c + 1) * 128], nega[:]))
                    v.tensor_tensor_scan(acs[:], ons16[:], acv[:], 0.0,
                                         ALU.mult, ALU.add).then_inc(S["acs"], 1)
                    v.wait_ge(S["mt"], ci + 1)
                    v.tensor_tensor(Msk[:], psSa[:, 0:128], trius[:], ALU.mult).then_inc(S["msk"], 1)
                    v.wait_ge(S["bt"], 2 * (ci + 1))
                    v.tensor_copy(BTs[:], psTa[:, 0:128]).then_inc(S["btv"], 1)
                    v.tensor_copy(sgt[:], psTb[:]).then_inc(S["sgte"], 1)
                    if c >= 1:
                        v.tensor_scalar_mul(diag[:], i16s[:], ell[:, 0:1]).then_inc(S["dgv"], 1)
                        v.wait_ge(S["ebe"], ne)
                        for h in range(16):
                            o = h * 64
                            r = v.scalar_tensor_tensor(
                                prevb[:, cp * 1024 + o: cp * 1024 + o + 64],
                                prevb[:, (1 - cp) * 1024 + o: (1 - cp) * 1024 + o + 64],
                                Ebc[:, h:h + 1],
                                stb[:, (1 - cp) * 1024 + o: (1 - cp) * 1024 + o + 64],
                                ALU.mult, ALU.add)
                        fence(r)
                        v.tensor_copy(prevbf[:], prevb[:, cp * 1024:(cp + 1) * 1024]).then_inc(S["prv"], 1)
                    v.wait_ge(S["dtt"], ci + 1)
                    for h in range(16):
                        hp = h % 2
                        v.wait_ge(S["dmm"], ci * 16 + h + 1)
                        v.tensor_scalar_min(LTm[:, hp * 128:(hp + 1) * 128],
                                            psD[hp][:, 0:128], 0.0).then_inc(S["ltm"], 1)
                        v.wait_ge(S["lte"], ci * 16 + h + 1)
                        fence(v.tensor_tensor(ddv[:, hp:hp + 1],
                                              LTe[:, hp * 128 + 127: hp * 128 + 128],
                                              dtT[:, h:h + 1], ALU.mult))
                        v.tensor_tensor(Ylh[:, hp * 128:(hp + 1) * 128], Msk[:],
                                        LTe[:, hp * 128:(hp + 1) * 128], ALU.mult)
                        v.wait_ge(S["ht"], 16 * ci + h + 1)
                        r = v.tensor_scalar_mul(srhp[:, h * 64:(h + 1) * 64],
                                                psT_[hp][:, 0:64], ddv[:, hp:hp + 1])
                        fence(r)
                        v.tensor_copy(vcs[:, 0:1], vcs[:, 1:2]).then_inc(S["ylh"], 1)
                    v.wait_ge(S["sm"], 2 * (ci + 1))
                    v.tensor_copy(stb[:, cp * 1024: cp * 1024 + 512], psSa[:])
                    fence(v.tensor_copy(stb[:, cp * 1024 + 512: cp * 1024 + 1024], psSb[:]))
                    v.tensor_copy(vcs[:, 0:1], vcs[:, 1:2]).then_inc(S["ste"], 1)
                    v.wait_ge(S["yd"], ci + 1)
                    v.tensor_copy(ysb[:, 0:512], psY0[:])
                    fence(v.tensor_copy(ysb[:, 512:1024], psY1[:]))
                    v.tensor_copy(vcs[:, 0:1], vcs[:, 1:2]).then_inc(S["yev"], 1)
                    if c >= 1:
                        v.wait_ge(S["yo"], 2 * ne)
                        v.wait_ge(S["act"], ci + 1)
                        for h in range(16):
                            o = h * 64
                            ps = psS_[h // 8]
                            r = v.scalar_tensor_tensor(ysb[:, o:o + 64],
                                                       ps[:, (h % 8) * 64:(h % 8) * 64 + 64],
                                                       EA[:, h:h + 1], ysb[:, o:o + 64],
                                                       ALU.mult, ALU.add)
                        fence(r)
                        v.tensor_copy(vcs[:, 0:1], vcs[:, 1:2]).then_inc(S["ya"], 1)
                    v.wait_ge(S["hde"], 16 * (ci + 1))
                    for h in range(16):
                        o = h * 64
                        r = v.scalar_tensor_tensor(y2b[:, o:o + 64], hcpp[:, o:o + 64],
                                                   dps[:, u * 16 + h: u * 16 + h + 1],
                                                   ysb[:, o:o + 64], ALU.mult, ALU.add)
                    fence(r)
                    v.wait_ge(S["sgte"], ci + 1)
                    fence(v.tensor_tensor(ygb[:], y2b[:], sgt[:], ALU.mult))
                    v.tensor_copy(vcs[:, 0:1], vcs[:, 1:2]).then_inc(S["yg"], 1)
                    v.wait_ge(S["sq"], ci + 1)
                    pb = ci % 2
                    if ci >= 2:
                        v.wait_ge(S["yst0" if pb == 0 else "yst1"], 16 * (ci // 2))
                    v.tensor_scalar_mul(ybf[:, pb * 1024:(pb + 1) * 1024], ygb[:],
                                        rgs[:, 0:1]).then_inc(S["ybo"], 1)

        @block.scalar
        def _(s):
            nfes = [0]

            def fens(r):
                r.then_inc(S["fes"], 1)
                nfes[0] += 1
                s.wait_ge(S["fes"], nfes[0])

            for u in range(NU):
                s.wait_ge(S["ucl"], 16 * 4 * (u + 1))
                s.activation(ell[:, 1:2], alg[:], AF.Exp).then_inc(S["el"], 1)
                for c in range(NCH):
                    ci = u * NCH + c
                    ne = ne_of(u, c)
                    s.wait_ge(S["bt"], 2 * ci + 1)
                    s.copy(dtT[:], psDa[:, 256:272])
                    fens(s.copy(acsT[:], psDa[:, 272:288]))
                    s.copy(scs[:, 0:1], scs[:, 1:2]).then_inc(S["dtt"], 1)
                    s.activation(EA[:], acsT[:], AF.Exp).then_inc(S["act"], 1)
                    if c >= 1:
                        s.wait_ge(S["dgv"], ne)
                    s.activation(ell[:, 0:1], acs[:, 127:128], AF.Exp).then_inc(S["el"], 1)
                    if c >= 1:
                        s.wait_ge(S["ebm"], ne)
                        s.copy(Ebc[:], psDa[:, 288:304]).then_inc(S["ebe"], 1)
                    for h in range(16):
                        hp = h % 2
                        s.wait_ge(S["ltm"], ci * 16 + h + 1)
                        s.activation(LTe[:, hp * 128:(hp + 1) * 128],
                                     LTm[:, hp * 128:(hp + 1) * 128], AF.Exp).then_inc(S["lte"], 1)
                        s.wait_ge(S["ht"], 16 * ci + h + 1)
                        s.wait_ge(S["dtt"], ci + 1)
                        s.activation(hdtp[:, h * 64:(h + 1) * 64], psT_[hp][:, 0:64],
                                     AF.Identity, scale=dtT[:, h:h + 1])
                        s.copy(hcpp[:, h * 64:(h + 1) * 64], psT_[hp][:, 0:64]).then_inc(S["hde"], 1)
                    s.wait_ge(S["yg"], ci + 1)
                    s.activation(jnk[:], ygb[:], AF.Square, accum_out=ssq[:])
                    fens(s.activation(rgs[:, 1:2], ssq[:], AF.Ln, scale=1.0 / 1024.0, bias=epsc[:]))
                    s.activation(rgs[:, 0:1], rgs[:, 1:2], AF.Exp, scale=-0.5).then_inc(S["sq"], 1)

        @block.tensor
        def _(t):
            t.wait_ge(S["cld"], 16 * 7)
            for u in range(NU):
                t.wait_ge(S["ucl"], 16 * 4 * (u + 1))
                for c in range(NCH):
                    ci = u * NCH + c
                    p = ci % 2
                    cp = c % 2
                    ne = ne_of(u, c)
                    t.wait_ge(S["acs"], ci + 1)
                    if ci >= 1:
                        t.wait_ge(S["ltm"], 16 * ci)
                        t.wait_ge(S["dtt"], ci)
                        if ci >= NCH or (ci % NCH) >= 1 or True:
                            pass
                    t.transpose(psDa[:, 256:272], dtu[:, c * 128:(c + 1) * 128], idfs[0:16, 0:16])
                    t.transpose(psDa[:, 272:288], acs[:], idfs[0:16, 0:16]).then_inc(S["bt"], 1)
                    if ci >= 1:
                        t.wait_ge(S["ste"], ci)
                        t.wait_ge(S["msk"], ci)
                        if (ci % NCH) != 0:
                            t.wait_ge(S["ya"], ne_of(*divmod(ci - 1, NCH)))
                        elif u >= 1:
                            t.wait_ge(S["ya"], ne_of(u - 1, NCH - 1))
                    t.matmul(psSa[:, 0:128], Bb[:, c * 128:(c + 1) * 128],
                             Cb[:, c * 128:(c + 1) * 128], start=True, stop=True).then_inc(S["mt"], 1)
                    t.wait_ge(s_hl[p], 16 * (ci // 2 + 1))
                    t.wait_ge(s_sl[p], 16 * (ci // 2 + 1))
                    if ci >= 1:
                        t.wait_ge(S["hde"], 16 * ci)
                        t.wait_ge(S["ylh"], 16 * ci)
                        t.wait_ge(S["btv"], ci)
                        t.wait_ge(S["sgte"], ci)
                    t.transpose(psTa[:, 0:128], Bb[:, c * 128:(c + 1) * 128], idbs[:])
                    for r8 in range(8):
                        rr = t.transpose(psTb[:, r8 * 128:(r8 + 1) * 128],
                                         sgc[:, p * 1024 + r8 * 128: p * 1024 + (r8 + 1) * 128],
                                         idbs[:])
                    rr.then_inc(S["bt"], 1)
                    if c >= 1:
                        t.wait_ge(S["dgv"], ne)
                        if ne >= 2:
                            t.wait_ge(S["ebe"], ne - 1)
                            t.wait_ge(S["prv"], ne - 1)
                        t.matmul(psDa[:, 288:304], ons16[:], diag[:],
                                 start=True, stop=True).then_inc(S["ebm"], 1)
                    t.wait_ge(S["dml"], 16 + 32 * (ci + 1))
                    t.wait_ge(S["dtt"], ci + 1)
                    if c >= 1:
                        t.wait_ge(S["ebe"], ne)
                    for h in range(16):
                        hp = h % 2
                        if ci * 16 + h >= 2:
                            t.wait_ge(S["ltm"], ci * 16 + h - 1)
                        t.matmul(psD[hp][:, 0:128], dop[:, h * 128:(h + 1) * 128],
                                 dop[:, 2048 + h * 128: 2048 + (h + 1) * 128],
                                 start=True, stop=True).then_inc(S["dmm"], 1)
                        if ci * 16 + h >= 2:
                            t.wait_ge(S["hde"], ci * 16 + h - 1)
                            t.wait_ge(S["ylh"], ci * 16 + h - 1)
                        if h == 0:
                            t.wait_ge(S["btv"], ci + 1)
                        if h == 1:
                            t.wait_ge(S["sgte"], ci + 1)
                        t.transpose(psT_[hp][:, 0:64],
                                    hid3[(h % 2) * 64:(h % 2) * 64 + 64, p, h // 2, :],
                                    idb64s[(h % 2) * 64:(h % 2) * 64 + 64, :]).then_inc(S["ht"], 1)
                        t.wait_ge(S["ylh"], ci * 16 + h + 1)
                        t.wait_ge(S["hde"], ci * 16 + h + 1)
                        if h == 0:
                            t.wait_ge(S["yev"], ci)
                        r = t.matmul(psY[h // 8][:, (h % 8) * 64:(h % 8) * 64 + 64],
                                     Ylh[:, hp * 128:(hp + 1) * 128],
                                     hdtp[:, h * 64:(h + 1) * 64],
                                     start=True, stop=True)
                        if h == 15:
                            r.then_inc(S["yd"], 1)
                    t.wait_ge(S["ylh"], 16 * (ci + 1))
                    t.wait_ge(S["btv"], ci + 1)
                    t.wait_ge(S["msk"], ci + 1)
                    t.matmul(psSa[:], BTs[:], srhp[:, 0:512], start=True, stop=True)
                    t.matmul(psSb[:], BTs[:], srhp[:, 512:1024], start=True, stop=True).then_inc(S["sm"], 2)
                    if c >= 1:
                        t.wait_ge(S["ste"], ci + 1)
                        t.wait_ge(S["prv"], ne)
                        t.matmul(psSa[:], Cb[:, c * 128:(c + 1) * 128],
                                 prevbf[:, 0:512], start=True, stop=True)
                        t.matmul(psSb[:], Cb[:, c * 128:(c + 1) * 128],
                                 prevbf[:, 512:1024],
                                 start=True, stop=True).then_inc(S["yo"], 2)
    return nc



def build_final_launch(T: int = TC):
    TH = T // 2
    nc = bass.Bass("TRN2", target_bir_lowering=False)
    ygf = nc.dram_tensor("ygf", [8192, T], BF16, kind="ExternalInput").ap()
    xn1f = nc.dram_tensor("xn1f", [D, T], BF16, kind="ExternalInput").ap()
    x1f = nc.dram_tensor("x1f", [D, T], F32, kind="ExternalInput").ap()
    OutWT = nc.dram_tensor("OutWT", [128, 32, 64, 128], F32, kind="ExternalInput").ap()
    A1 = nc.dram_tensor("A1", [D, D], I32, kind="ExternalInput").ap()
    ws1 = nc.dram_tensor("ws1", [128, 32], F32, kind="ExternalInput").ap()
    wz1 = nc.dram_tensor("wz1", [128, 32], F32, kind="ExternalInput").ap()
    bias1f = nc.dram_tensor("bias1f", [128, 32], F32, kind="ExternalInput").ap()
    gnwf = nc.dram_tensor("gnwf", [128, 64], F32, kind="ExternalInput").ap()
    identb = nc.dram_tensor("identb", [128, 128], BF16, kind="ExternalInput").ap()
    outf = nc.dram_tensor("outf", [D, T], F32, kind="ExternalOutput").ap()
    w1scr = nc.dram_tensor("w1scr", [128, 32, 32, 128], BF16).ap()
    oacc = nc.dram_tensor("oacc", [D, T], F32).ap()

    with ExitStack() as ctx:
        bigb = ctx.enter_context(nc.sbuf_tensor("bigb", [128, 64 * TH], BF16))
        ygh = bigb[:]
        xnh = bigb[:, 0:32 * TH]
        qvb = bigb[:].bitcast(I32)
        wsb = ctx.enter_context(nc.sbuf_tensor("wsb", [128, 2 * 64 * 128], BF16))
        w1sb = ctx.enter_context(nc.sbuf_tensor("w1sb", [128, 2 * 32 * 128], BF16))
        ws1s = ctx.enter_context(nc.sbuf_tensor("ws1s", [128, 32], F32))
        wz1s = ctx.enter_context(nc.sbuf_tensor("wz1s", [128, 32], F32))
        zs1 = ctx.enter_context(nc.sbuf_tensor("zs1", [128, 32], F32))
        b1s = ctx.enter_context(nc.sbuf_tensor("b1s", [128, 32], F32))
        gnws = ctx.enter_context(nc.sbuf_tensor("gnws", [128, 64], F32))
        idb = ctx.enter_context(nc.sbuf_tensor("idb", [128, 128], BF16))
        tmpb = ctx.enter_context(nc.sbuf_tensor("tmpb", [128, TH], BF16))
        oa = ctx.enter_context(nc.sbuf_tensor("oa", [128, 2 * TH], F32))
        ob = ctx.enter_context(nc.sbuf_tensor("ob", [128, 2 * TH], F32))
        dq = oa[:].bitcast(BF16)[:, 0:D]
        w1T = ob[:].bitcast(BF16)[:, 0:D]
        xs = ctx.enter_context(nc.sbuf_tensor("xs", [128, 2 * TH], F32))
        vcs = ctx.enter_context(nc.sbuf_tensor("vcs", [128, 2], F32))
        psA0 = ctx.enter_context(nc.psum_tensor("psA0", [128, 512], F32))
        psA1 = ctx.enter_context(nc.psum_tensor("psA1", [128, 512], F32))
        psB0 = ctx.enter_context(nc.psum_tensor("psB0", [128, 512], F32))
        psB1 = ctx.enter_context(nc.psum_tensor("psB1", [128, 512], F32))
        psT0 = ctx.enter_context(nc.psum_tensor("psT0", [128, 1024], BF16))
        psT1 = ctx.enter_context(nc.psum_tensor("psT1", [128, 1024], BF16))
        sems = {}
        for nm in ["cld", "ql0", "ql1", "dqv", "tp", "tx", "w1st", "ygl", "gnf",
                   "wl0", "wl1", "mma", "eva", "oast", "xnl", "w1l0", "w1l1",
                   "x1l0", "x1l1", "oal0", "oal1", "mmf", "evf", "ost", "fen"]:
            sems[nm] = ctx.enter_context(nc.semaphore("s_" + nm))
        block = ctx.enter_context(nc.Block())
        S = sems
        psT = [psT0, psT1]
        s_ql = [S["ql0"], S["ql1"]]
        s_wl = [S["wl0"], S["wl1"]]
        s_w1l = [S["w1l0"], S["w1l1"]]
        s_x1l = [S["x1l0"], S["x1l1"]]
        s_oal = [S["oal0"], S["oal1"]]

        @block.sync
        def _(sync):
            for ap_, sb in [(ws1, ws1s), (wz1, wz1s), (bias1f, b1s), (gnwf, gnws),
                            (identb, idb)]:
                sync.dma_start(out=sb[:], in_=ap_[:]).then_inc(S["cld"], 16)
            for ncn in range(32):
                sync.wait_ge(S["tx"], 32 * (ncn + 1))
                sync.dma_start(out=w1scr[:, ncn], in_=w1T[:].rearrange("p (a b) -> p a b", b=128)
                               ).then_inc(S["w1st"], 16)
            for th in range(2):
                if th == 1:
                    sync.wait_ge(S["mma"], 32)
                for kc in range(64):
                    sync.dma_start(out=ygh[:, kc * TH:(kc + 1) * TH],
                                   in_=ygf[kc * 128:(kc + 1) * 128, th * TH:(th + 1) * TH]
                                   ).then_inc(S["ygl"], 16)
                for nb in range(32):
                    i = th * 32 + nb
                    sync.wait_ge(S["eva"], i + 1)
                    sync.dma_start(out=oacc[nb * 128:(nb + 1) * 128, th * TH:(th + 1) * TH],
                                   in_=oa[:, (i % 2) * TH:(i % 2 + 1) * TH]).then_inc(S["oast"], 16)
            for th in range(2):
                if th == 0:
                    sync.wait_ge(S["mma"], 64)
                else:
                    sync.wait_ge(S["mmf"], 32)
                for kc in range(32):
                    sync.dma_start(out=xnh[:, kc * TH:(kc + 1) * TH],
                                   in_=xn1f[kc * 128:(kc + 1) * 128, th * TH:(th + 1) * TH]
                                   ).then_inc(S["xnl"], 16)
                for nb in range(32):
                    i = th * 32 + nb
                    p = i % 2
                    if i >= 2:
                        sync.wait_ge(S["evf"], i - 1)
                    sync.dma_start(out=xs[:, p * TH:(p + 1) * TH],
                                   in_=x1f[nb * 128:(nb + 1) * 128, th * TH:(th + 1) * TH]
                                   ).then_inc(s_x1l[p], 16)
                    sync.wait_ge(S["oast"], 16 * (th * 32 + nb + 1) if th == 0 else 16 * 64)
                    sync.dma_start(out=ob[:, p * TH:(p + 1) * TH],
                                   in_=oacc[nb * 128:(nb + 1) * 128, th * TH:(th + 1) * TH]
                                   ).then_inc(s_oal[p], 16)
                    sync.wait_ge(S["evf"], i + 1)
                    sync.dma_start(out=outf[nb * 128:(nb + 1) * 128, th * TH:(th + 1) * TH],
                                   in_=oa[:, (i % 2) * TH:(i % 2 + 1) * TH]).then_inc(S["ost"], 16)
            sync.wait_ge(S["ost"], 16 * 64)

        @block.gpsimd
        def _(g):
            for ncn in range(32):
                p = ncn % 2
                if ncn >= 2:
                    g.wait_ge(S["dqv"], ncn - 1)
                g.dma_start(out=qvb[:, p * D:(p + 1) * D],
                            in_=A1[ncn * 128:(ncn + 1) * 128, :]).then_inc(s_ql[p], 16)
            for th in range(2):
                for nb in range(32):
                    i = th * 32 + nb
                    p = i % 2
                    if i >= 2:
                        g.wait_ge(S["mma"], i - 1)
                    g.dma_start(out=wsb[:, p * 8192:(p + 1) * 8192].rearrange("p (a b) -> p a b", b=128),
                                in_=OutWT[:, nb]).then_inc(s_wl[p], 16)
            for th in range(2):
                for nb in range(32):
                    i = th * 32 + nb
                    p = i % 2
                    if i == 0:
                        g.wait_ge(S["w1st"], 16 * 32)
                    if i >= 2:
                        g.wait_ge(S["mmf"], i - 1)
                    g.dma_start(out=w1sb[:, p * 4096:(p + 1) * 4096].rearrange("p (a b) -> p a b", b=128),
                                in_=w1scr[:, nb]).then_inc(s_w1l[p], 16)

        @block.vector
        def _(v):
            nfen = [0]

            def fence(r):
                r.then_inc(S["fen"], 1)
                nfen[0] += 1
                v.wait_ge(S["fen"], nfen[0])

            v.wait_ge(S["cld"], 16 * 5)
            fence(v.tensor_tensor(zs1[:], wz1s[:], ws1s[:], ALU.mult))
            fence(v.tensor_scalar_mul(zs1[:], zs1[:], -1.0))
            for ncn in range(32):
                p = ncn % 2
                v.wait_ge(s_ql[p], 16 * (ncn // 2 + 1))
                if ncn >= 1:
                    v.wait_ge(S["tp"], 32 * ncn)
                q3 = qvb[:, p * D:(p + 1) * D].rearrange("p (a b) -> p a b", b=4)
                fence(v.tensor_single_scalar(q3[:, :, 1], q3[:, :, 1], 6,
                                             ALU.logical_shift_right))
                fence(v.tensor_single_scalar(qvb[:, p * D:(p + 1) * D],
                                             qvb[:, p * D:(p + 1) * D], 63, ALU.bitwise_and))
                for gx in range(4):
                    r = v.tensor_scalar(dq[:, gx * 1024:(gx + 1) * 1024],
                                        qvb[:, p * D + gx * 1024: p * D + (gx + 1) * 1024],
                                        ws1s[:, ncn:ncn + 1], zs1[:, ncn:ncn + 1],
                                        ALU.mult, ALU.add)
                r.then_inc(S["dqv"], 1)
            for th in range(2):
                v.wait_ge(S["ygl"], 16 * 64 * (th + 1))
                for kc in range(64):
                    fence(v.tensor_scalar_mul(tmpb[:], ygh[:, kc * TH:(kc + 1) * TH],
                                              gnws[:, kc:kc + 1]))
                    fence(v.tensor_copy(ygh[:, kc * TH:(kc + 1) * TH], tmpb[:]))
                v.tensor_copy(vcs[:, 0:1], vcs[:, 1:2]).then_inc(S["gnf"], 1)
                for nb in range(32):
                    i = th * 32 + nb
                    v.wait_ge(S["mma"], i + 1)
                    if i >= 2:
                        v.wait_ge(S["oast"], 16 * (i - 1))
                    v.tensor_copy(oa[:, (i % 2) * TH: (i % 2) * TH + 512], psA0[:])
                    r = v.tensor_copy(oa[:, (i % 2) * TH + 512:(i % 2) * TH + 1024], psA1[:])
                    r.then_inc(S["eva"], 1)
            for th in range(2):
                for nb in range(32):
                    i = th * 32 + nb
                    p = i % 2
                    v.wait_ge(S["mmf"], i + 1)
                    v.wait_ge(s_oal[p], 16 * (i // 2 + 1))
                    v.wait_ge(s_x1l[p], 16 * (i // 2 + 1))
                    if i < 2:
                        v.wait_ge(S["oast"], 16 * (63 + i))
                    if i >= 2:
                        v.wait_ge(S["ost"], 16 * (i - 1))
                    for bk, ps in [(0, psB0), (1, psB1)]:
                        o0 = p * TH + bk * 512
                        fence(v.tensor_scalar_add(oa[:, o0:o0 + 512], ps[:],
                                                  b1s[:, nb:nb + 1]))
                    fence(v.tensor_tensor(oa[:, p * TH:(p + 1) * TH], oa[:, p * TH:(p + 1) * TH],
                                          ob[:, p * TH:(p + 1) * TH], ALU.add))
                    fence(v.tensor_tensor(oa[:, p * TH:(p + 1) * TH], oa[:, p * TH:(p + 1) * TH],
                                          xnh[:, nb * TH:(nb + 1) * TH], ALU.add))
                    r = v.tensor_tensor(oa[:, p * TH:(p + 1) * TH], oa[:, p * TH:(p + 1) * TH],
                                        xs[:, p * TH:(p + 1) * TH], ALU.add)
                    r.then_inc(S["evf"], 1)

        @block.scalar
        def _(s):
            for ncn in range(32):
                for kc in range(32):
                    s.wait_ge(S["tp"], ncn * 32 + kc + 1)
                    s.copy(w1T[:, kc * 128:(kc + 1) * 128],
                           psT[kc % 2][:, 0:128]).then_inc(S["tx"], 1)

        @block.tensor
        def _(t):
            t.wait_ge(S["cld"], 16 * 5)
            for ncn in range(32):
                t.wait_ge(S["dqv"], ncn + 1)
                if ncn >= 1:
                    t.wait_ge(S["w1st"], 16 * ncn)
                for kc in range(32):
                    if ncn * 32 + kc >= 2:
                        t.wait_ge(S["tx"], ncn * 32 + kc - 1)
                    t.transpose(psT[kc % 2][:, 0:128], dq[:, kc * 128:(kc + 1) * 128],
                                idb[:]).then_inc(S["tp"], 1)
            for th in range(2):
                t.wait_ge(S["gnf"], th + 1)
                for nb in range(32):
                    i = th * 32 + nb
                    p = i % 2
                    t.wait_ge(s_wl[p], 16 * (i // 2 + 1))
                    if i >= 1:
                        t.wait_ge(S["eva"], i)
                    for kc in range(64):
                        lhsT = wsb[:, p * 8192 + kc * 128: p * 8192 + (kc + 1) * 128]
                        t.matmul(psA0[:], lhsT, ygh[:, kc * TH: kc * TH + 512],
                                 start=(kc == 0), stop=(kc == 63))
                        r = t.matmul(psA1[:], lhsT, ygh[:, kc * TH + 512: kc * TH + 1024],
                                     start=(kc == 0), stop=(kc == 63))
                    r.then_inc(S["mma"], 1)
            for th in range(2):
                t.wait_ge(S["xnl"], 16 * 32 * (th + 1))
                for nb in range(32):
                    i = th * 32 + nb
                    p = i % 2
                    t.wait_ge(s_w1l[p], 16 * (i // 2 + 1))
                    if i >= 1:
                        t.wait_ge(S["evf"], i)
                    for kc in range(32):
                        lhsT = w1sb[:, p * 4096 + kc * 128: p * 4096 + (kc + 1) * 128]
                        t.matmul(psB0[:], lhsT, xnh[:, kc * TH: kc * TH + 512],
                                 start=(kc == 0), stop=(kc == 31))
                        r = t.matmul(psB1[:], lhsT, xnh[:, kc * TH + 512: kc * TH + 1024],
                                     start=(kc == 0), stop=(kc == 31))
                    r.then_inc(S["mmf"], 1)
    return nc


def _stage_final(out_proj_w, qweight, weight_scale, weight_zero_point, bias, gnorm_w):
    W4 = np.asarray(out_proj_w, np.float32).T.reshape(64, 128, 32, 128)
    OutWT = np.ascontiguousarray(W4.transpose(1, 2, 0, 3))
    q = np.asarray(qweight)
    cols = np.empty(4096, np.int64)
    for k in range(4096):
        g, j = divmod(k, 4)
        cols[k] = 3 * g + (0 if j == 0 else (0 if j == 1 else (1 if j == 2 else 2)))
    A1 = np.ascontiguousarray(q[:, cols]).astype(np.int32)
    ws1 = np.ascontiguousarray(np.asarray(weight_scale, np.float32).reshape(32, 128).T)
    wz1 = np.ascontiguousarray(np.asarray(weight_zero_point, np.float32).reshape(32, 128).T)
    b1 = np.ascontiguousarray(np.asarray(bias, np.float32).reshape(32, 128).T)
    gnw = np.ascontiguousarray(np.asarray(gnorm_w, np.float32).reshape(64, 128).T)
    return OutWT, A1, ws1, wz1, b1, gnw


def run_final(yg_shards, xn1_shards, x1_shards, OutWT, A1, ws1, wz1, b1, gnw, identb):
    nc = build_final_launch()
    in_maps = [{"ygf": a, "xn1f": b, "x1f": c, "OutWT": OutWT, "A1": A1,
                "ws1": ws1, "wz1": wz1, "bias1f": b1, "gnwf": gnw, "identb": identb}
               for a, b, c in zip(yg_shards, xn1_shards, x1_shards)]
    res = run_bass_kernel_spmd(nc, in_maps, core_ids=list(range(len(in_maps))))
    return [r["outf"] for r in res.results]



def kernel(L0_attn_data, L0_attn_weight, L0_attn_mask, L0_ffn_weight_q,
           L0_ffn_scales_and_zeros, L1_attn_norm_w, L1_attn_in_proj_w,
           L1_attn_conv_w, L1_attn_conv_b, L1_attn_dt_bias, L1_attn_A_log,
           L1_attn_Dp, L1_attn_gnorm_w, L1_attn_out_proj_w, L1_ffn_qweight,
           L1_ffn_weight_scale, L1_ffn_weight_zero_point, L1_ffn_bias):
    x0 = np.ascontiguousarray(np.asarray(L0_attn_data, np.float32).reshape(16384, D))
    x0_shards = [np.ascontiguousarray(x0[c * TC:(c + 1) * TC]) for c in range(NCORES)]

    xn0_tm = run_ln0(x0_shards)

    Ebm, Pbm, identb = _make_l0_consts()
    szr = _stage_szr(L0_ffn_scales_and_zeros)
    o0 = run_l0([_t(s) for s in xn0_tm], np.asarray(L0_attn_weight, np.float32),
                np.asarray(L0_ffn_weight_q, np.int32),
                np.asarray(L0_attn_mask, np.int32), szr, Ebm, Pbm, identb)

    x1_tm, xn1_tm, h_tm = run_ln1(x0_shards, [_t(s) for s in o0])

    h_full = np.concatenate(h_tm, axis=0)
    hf_shards = []
    for c in range(NCORES):
        b, hb = divmod(c, 2)
        base = b * S + hb * TC
        if hb == 0:
            halo = np.zeros((128, D), dtype=BF16NP)
        else:
            halo = h_full[base - 128: base]
        hf_shards.append(_t(np.concatenate([halo, h_full[base: base + TC]], axis=0)))
    WinT = _stage_proj_w(L1_attn_in_proj_w)
    cw, cb, nw, db = _stage_conv(L1_attn_conv_w, L1_attn_conv_b, L1_attn_norm_w,
                                 L1_attn_dt_bias)
    dres = run_proj(hf_shards, WinT, nw, cw, cb, db)

    A_log = np.asarray(L1_attn_A_log, np.float32)
    Dp = np.asarray(L1_attn_Dp, np.float32)
    hid_b, sg_b, B_b, C_b, dt_b = [], [], [], [], []
    for b in range(B_SZ):
        hid_b.append(np.concatenate([dres[2 * b]["hid"], dres[2 * b + 1]["hid"]], axis=2))
        sg_b.append(np.concatenate([dres[2 * b]["sg"], dres[2 * b + 1]["sg"]], axis=2))
        B_b.append(np.concatenate([dres[2 * b]["Bc"], dres[2 * b + 1]["Bc"]], axis=2))
        C_b.append(np.concatenate([dres[2 * b]["Cc"], dres[2 * b + 1]["Cc"]], axis=2))
        dt_b.append(np.concatenate([dres[2 * b]["dto"], dres[2 * b + 1]["dto"]], axis=1))
    units = [(b, g) for b in range(B_SZ) for g in range(8)]
    idf = np.eye(128, dtype=np.float32)
    idb128 = np.eye(128, dtype=BF16NP)
    triu = np.triu(np.ones((128, 128), np.float32))
    ones16 = np.ones((16, 128), np.float32)
    i16 = np.eye(16, dtype=np.float32)
    idb64 = np.vstack([np.eye(64, dtype=BF16NP)] * 2)
    negrow = np.vstack([np.zeros((1, 2048), np.float32), -np.ones((1, 2048), np.float32)])
    in_maps = []
    for c in range(NCORES):
        us = units[4 * c: 4 * c + 4]
        ins = {
            "hid4": np.stack([hid_b[b][8 * g: 8 * g + 8] for b, g in us]),
            "sg4": np.stack([sg_b[b][8 * g: 8 * g + 8] for b, g in us]),
            "B4": np.stack([B_b[b][g] for b, g in us]),
            "C4": np.stack([C_b[b][g] for b, g in us]),
            "dt4": np.stack([dt_b[b][16 * g: 16 * g + 16] for b, g in us]),
            "alog": np.stack([A_log[16 * g: 16 * g + 16][:, None] for b, g in us]),
            "dpb": np.concatenate([np.tile(Dp[16 * g: 16 * g + 16], (128, 1))
                                   for b, g in us], axis=1),
            "idf": idf, "idbm": idb128, "triu": triu, "ones16": ones16,
            "i16": i16, "idb64": idb64, "negrow": negrow,
        }
        in_maps.append(ins)
    nc_e = build_ssd_launch()
    res_e = run_bass_kernel_spmd(nc_e, in_maps, core_ids=list(range(NCORES)))
    y_full = np.empty((16384, 8192), dtype=BF16NP)
    for c in range(NCORES):
        y4 = res_e.results[c]["y4"]
        for j, (b, g) in enumerate(units[4 * c: 4 * c + 4]):
            y_full[b * S:(b + 1) * S, g * 1024:(g + 1) * 1024] = y4[j].reshape(S, 1024)

    OutWT, A1, ws1, wz1, b1f, gnwf = _stage_final(
        L1_attn_out_proj_w, L1_ffn_qweight, L1_ffn_weight_scale,
        L1_ffn_weight_zero_point, L1_ffn_bias, L1_attn_gnorm_w)
    outs = run_final([_t(y_full[c * TC:(c + 1) * TC]) for c in range(NCORES)],
                     [_t(s) for s in xn1_tm], [_t(s) for s in x1_tm],
                     OutWT, A1, ws1, wz1, b1f, gnwf, idb128)
    out = np.concatenate([o.T for o in outs], axis=0)
    return np.ascontiguousarray(out.reshape(B_SZ, S, D))
